# version 01
# baseline (speedup 1.0000x reference)
import numpy as np
import ml_dtypes
import concourse.bass as bass
import concourse.mybir as mybir
from concourse.bass_utils import run_bass_kernel_spmd

F32 = mybir.dt.float32
BF16 = mybir.dt.bfloat16
I32 = mybir.dt.int32
AF = mybir.ActivationFunctionType
ALU = mybir.AluOpType
AX = mybir.AxisListType.X

BC = 4
T = 2048
D = 1024
NTT = 16
NCH = 4
TC = 512
KD = 8
KS = 256
EPS_LN = 1e-5
EPS_RMS = 1.1920929e-07

ENGS = ("sync", "tensor", "vector", "scalar", "gpsimd")
PARANOID = 0


class PB:

    def __init__(self):
        self.entries = {e: [] for e in ENGS}
        self.cnt = {}
        self.sems = {}

    def new_sem(self, name):
        self.cnt[name] = 0

    def wait(self, e, sem, val):
        if val <= 0:
            return
        self.entries[e].append([("wait", sem, val), None, 0])

    def op(self, e, fn, inc=None, n=1):
        ent = [fn, None, 0, None]
        self.entries[e].append(ent)
        if inc:
            self._mark(ent, inc, n)
        return ent

    def _mark(self, ent, sem, n):
        assert ent[1] is None, "entry already has an inc"
        ent[1] = sem
        ent[2] = n
        self.cnt[sem] += n
        ent[3] = self.cnt[sem]
        return self.cnt[sem]

    def mark(self, e, sem=None, n=1):
        sem = sem or ("m_" + e)
        for ent in reversed(self.entries[e]):
            if not isinstance(ent[0], tuple):
                if ent[1] is not None:
                    return ent[3]
                return self._mark(ent, sem, n)
        return 0

    def dep(self, e):
        ents = self.entries[e]
        for ent in reversed(ents):
            if not isinstance(ent[0], tuple):
                if ent[1] is None:
                    self._mark(ent, "m_" + e, 1)
                self.wait(e, ent[1], ent[3])
                return

    def dma(self, qe, out, in_):
        qsem = "q_" + qe
        self.op(qe, lambda eng, o=out, i=in_: eng.dma_start(out=o, in_=i),
                inc=qsem, n=16)
        return self.cnt[qsem]

    def barrier(self):
        for e in ENGS:
            ents = self.entries[e]
            for ent in reversed(ents):
                if not isinstance(ent[0], tuple):
                    if ent[1] is None:
                        self._mark(ent, "m_" + e, 1)
                    break
        targets = [(s, c) for s, c in self.cnt.items() if c > 0]
        for e in ENGS:
            for s, c in targets:
                self.wait(e, s, c)

    def emit(self, e, eng):
        for fn, sem, n, *_ in self.entries[e]:
            if isinstance(fn, tuple):
                _, s, v = fn
                eng.wait_ge(self.sems[s], v)
            else:
                ins = fn(eng)
                if sem:
                    ins.then_inc(self.sems[sem], n)


class Alloc:
    def __init__(self, nc):
        self.nc = nc
        self.scopes = []
        self.serial = 0

    def push(self):
        self.scopes.append([])

    def pop(self):
        for g in reversed(self.scopes.pop()):
            g.__exit__(None, None, None)

    def sb(self, name, shape, dtype):
        self.serial += 1
        g = self.nc.sbuf_tensor("%s_%d" % (name, self.serial), shape, dtype)
        h = g.__enter__()
        self.scopes[-1].append(g)
        return h

    def ps(self, name, shape, dtype):
        self.serial += 1
        g = self.nc.psum_tensor("%s_%d" % (name, self.serial), shape, dtype)
        h = g.__enter__()
        self.scopes[-1].append(g)
        return h


def host_prep(inputs):
    d = {}
    W_m = np.asarray(inputs["L0_attn_W_m"])
    Wm_r = W_m.reshape(KS, 4, D)
    d["WmT"] = np.ascontiguousarray(Wm_r.transpose(2, 1, 0).reshape(D, D), np.float32)
    W_p = np.asarray(inputs["L0_attn_W_proj"])
    Wp_r = W_p.reshape(D, KS, 2)
    d["WpT"] = np.ascontiguousarray(Wp_r.transpose(2, 1, 0).reshape(512, D), np.float32)
    vv = np.asarray(inputs["L0_attn_v_vec"])
    d["vv"] = np.ascontiguousarray(np.stack(
        [vv[0:128, 0, 0], vv[128:256, 0, 0], vv[0:128, 1, 0], vv[128:256, 1, 0]],
        axis=1), np.float32)
    d["osc"] = np.ascontiguousarray(
        np.asarray(inputs["L0_attn_out_scale"]).reshape(KD, 128).T, np.float32)
    d["rtT"] = np.ascontiguousarray(np.asarray(inputs["L0_ffn_router_weight"]).T, np.float32)
    d["guT"] = np.ascontiguousarray(
        np.asarray(inputs["L0_ffn_gate_up_proj"]).transpose(0, 2, 1), np.float32)
    d["dpT"] = np.ascontiguousarray(
        np.asarray(inputs["L0_ffn_down_proj"]).transpose(0, 2, 1), np.float32)
    d["winT"] = np.ascontiguousarray(np.asarray(inputs["L1_attn_w_in"]).T, np.float32)
    d["binS"] = np.ascontiguousarray(
        np.asarray(inputs["L1_attn_b_in"]).reshape(24, 128).T, np.float32)
    d["woutT"] = np.ascontiguousarray(np.asarray(inputs["L1_attn_w_out"]).T, np.float32)
    d["boutS"] = np.ascontiguousarray(
        np.asarray(inputs["L1_attn_b_out"]).reshape(8, 128).T, np.float32)
    d["wgT"] = np.ascontiguousarray(np.asarray(inputs["L1_ffn_w_gate"]).T, np.int32)
    d["wuT"] = np.ascontiguousarray(np.asarray(inputs["L1_ffn_w_up"]).T, np.int32)
    d["wdT"] = np.ascontiguousarray(np.asarray(inputs["L1_ffn_w_down"]).T, np.int32)
    d["eyeb"] = np.eye(128, dtype=ml_dtypes.bfloat16)
    d["eyef"] = np.eye(128, dtype=np.float32)
    d["onesf"] = np.ones((1, 128), dtype=np.float32)
    i = np.arange(128)[:, None]
    j = np.arange(384)[None, :]
    m_first = np.where((j >= i - 128) & (j <= i + 128), 0.0, -1e30)
    m_mid = np.where((j >= i) & (j <= i + 256), 0.0, -1e30)
    m_last = np.where((j >= i + 128) & (j < 384), 0.0, -1e30)
    d["masks"] = np.ascontiguousarray(
        np.stack([m_first, m_mid, m_last]).astype(np.float32))
    return d


HOST_KEYS = ["WmT", "WpT", "vv", "osc", "rtT", "guT", "dpT", "winT", "binS",
             "woutT", "boutS", "wgT", "wuT", "wdT", "eyeb", "eyef", "onesf", "masks"]



def emit_prologue(pb, al, H, S):
    pb.op("vector", lambda eng: eng.memset(S["epsc"][:, 0:1], EPS_LN))
    pb.op("vector", lambda eng: eng.memset(S["epsc"][:, 1:2], EPS_RMS), inc="m_vector")
    pb.dma("sync", S["eyeb"][:], H["eyeb"][:])
    pb.dma("sync", S["eyef"][:], H["eyef"][:])
    pb.dma("sync", S["onesf"][:], H["onesf"][:])
    pb.dma("sync", S["vv"][:], H["vv"][:])
    pb.dma("sync", S["osc"][:], H["osc"][:])
    pb.dma("sync", S["binS"][:], H["binS"][:])
    pb.dma("sync", S["boutS"][:], H["boutS"][:])
    pb.dma("sync", S["masks"][:].rearrange("p (m n) -> p m n", m=3),
           H["masks"].rearrange("m p n -> p m n"))
    stg = al.sb("prost", [128, 2 * D], F32)
    busy = [0, 0]
    jobs = ([(S["WmT"], k, H["WmT"].rearrange("(k p) f -> k p f", p=128), D) for k in range(KD)]
            + [(S["WpT"], k, H["WpT"].rearrange("(k p) f -> k p f", p=128), D) for k in range(4)]
            + [(S["rtT"], k, H["rtT"].rearrange("(k p) f -> k p f", p=128), 8) for k in range(KD)])
    for dstsb, k, src3, w in jobs:
        sl = k % 2
        dst = stg[:, sl * D: sl * D + w]
        pb.wait("sync", "m_vector", busy[sl])
        c = pb.dma("sync", dst, src3[k])
        pb.wait("vector", "q_sync", c)
        pb.op("vector", lambda eng, o=dstsb[:, k * w:(k + 1) * w], i=dst:
              eng.tensor_copy(o, i), inc="m_vector")
        busy[sl] = pb.cnt["m_vector"]
    pb.barrier()


def emit_weight_cache(pb, al, H, W):
    stg = al.sb("wcst", [128, 4 * 2048], F32)
    stgi = al.sb("wcsi", [128, 2048], I32)
    stgb = al.sb("wcsb", [128, 2 * 2048], BF16)
    jobs = [
        (H["guT"].rearrange("e (k p) f -> (e k) p f", p=128), W["guT_bf"], False, 2048),
        (H["dpT"].rearrange("e (k p) f -> (e k) p f", p=128), W["dpT_bf"], False, 1024),
        (H["winT"].rearrange("(k p) f -> k p f", p=128), W["winT_bf"], False, 3072),
        (H["woutT"].rearrange("(k p) f -> k p f", p=128), W["woutT_bf"], False, 1024),
        (H["wgT"].rearrange("(k p) f -> k p f", p=128), W["wgT_bf"], True, 3072),
        (H["wuT"].rearrange("(k p) f -> k p f", p=128), W["wuT_bf"], True, 3072),
        (H["wdT"].rearrange("(k p) f -> k p f", p=128), W["wdT_bf"], True, 1024),
    ]
    slot_busy = [0, 0, 0, 0]
    islot_busy = 0
    oslot_busy = [0, 0]
    si = 0
    oi = 0
    for src3, dstD, isint, w in jobs:
        n = src3.shape[0]
        dst3 = dstD.rearrange("(k p) f -> k p f", p=128)
        for t in range(n):
            for off in range(0, w, 2048):
                ww = min(2048, w - off)
                osl = oi % 2
                oi += 1
                ob = stgb[:, osl * 2048: osl * 2048 + ww]
                if isint:
                    pb.wait("sync", "m_vector", islot_busy)
                    c = pb.dma("sync", stgi[:, 0:ww], src3[t, :, off:off + ww])
                    sl = si % 4
                    si += 1
                    fdst = stg[:, sl * 2048: sl * 2048 + ww]
                    pb.wait("vector", "q_sync", c)
                    pb.wait("vector", "m_vector", slot_busy[sl])
                    pb.op("vector", lambda eng, o=fdst, i=stgi[:, 0:ww]:
                          eng.tensor_copy(o, i), inc="m_vector")
                    islot_busy = pb.cnt["m_vector"]
                    pb.wait("vector", "q_gpsimd", oslot_busy[osl])
                    pb.op("vector", lambda eng, o=ob, i=fdst:
                          eng.tensor_copy(o, i), inc="m_vector")
                    slot_busy[sl] = pb.cnt["m_vector"]
                else:
                    sl = si % 4
                    si += 1
                    fdst = stg[:, sl * 2048: sl * 2048 + ww]
                    pb.wait("sync", "m_vector", slot_busy[sl])
                    c = pb.dma("sync", fdst, src3[t, :, off:off + ww])
                    pb.wait("vector", "q_sync", c)
                    pb.wait("vector", "q_gpsimd", oslot_busy[osl])
                    pb.op("vector", lambda eng, o=ob, i=fdst:
                          eng.tensor_copy(o, i), inc="m_vector")
                    slot_busy[sl] = pb.cnt["m_vector"]
                pb.wait("gpsimd", "m_vector", pb.cnt["m_vector"])
                oslot_busy[osl] = pb.dma("gpsimd", dst3[t, :, off:off + ww], ob)
    pb.barrier()


def emit_l0_chunk_A(pb, S, x_dram, s_scr, b, c):
    st = S["stat"]
    for i in range(4):
        gt = c * 4 + i
        xt = S["xld"][:, i * D:(i + 1) * D]
        pb.wait("sync", "m_vector", S["_xld_busy"][i])
        cload = pb.dma("sync", xt, x_dram[b, gt * 128:(gt + 1) * 128, :])
        pb.wait("vector", "q_sync", cload)
        pb.op("vector", lambda eng, o=st[:, 0 + i:1 + i], i_=xt:
              eng.reduce_sum(o, i_, axis=AX), inc="m_vector")
        pb.wait("scalar", "q_sync", cload)
        pb.op("scalar", lambda eng, o=S["sqdump"][:], i_=xt, a=st[:, 4 + i:5 + i]:
              eng.activation(o, i_, AF.Square, bias=0.0, scale=1.0, accum_out=a),
              inc="m_scalar")
    pb.wait("vector", "m_scalar", pb.cnt["m_scalar"])
    pb.dep("vector")
    pb.op("vector", lambda eng: eng.tensor_scalar(st[:, 8:12], st[:, 0:4], 1.0 / D, None, ALU.mult))
    pb.dep("vector")
    pb.op("vector", lambda eng: eng.tensor_tensor(st[:, 12:16], st[:, 8:12], st[:, 8:12], ALU.mult))
    pb.op("vector", lambda eng: eng.tensor_scalar(st[:, 0:4], st[:, 4:8], 1.0 / D, None, ALU.mult))
    pb.dep("vector")
    pb.op("vector", lambda eng: eng.tensor_tensor(st[:, 12:16], st[:, 0:4], st[:, 12:16], ALU.subtract),
          inc="m_vector")
    pb.wait("scalar", "m_vector", pb.cnt["m_vector"])
    pb.op("scalar", lambda eng: eng.activation(st[:, 16:20], st[:, 12:16], AF.Sqrt,
                                               bias=S["epsc"][:, 0:1], scale=1.0), inc="m_scalar")
    pb.wait("vector", "m_scalar", pb.cnt["m_scalar"])
    pb.op("vector", lambda eng: eng.reciprocal(st[:, 16:20], st[:, 16:20]))
    pb.dep("vector")
    pb.op("vector", lambda eng: eng.tensor_tensor(st[:, 20:24], st[:, 12:16], st[:, 16:20], ALU.mult))
    pb.dep("vector")
    pb.op("vector", lambda eng: eng.tensor_tensor(st[:, 20:24], st[:, 20:24], st[:, 16:20], ALU.mult),
          inc="m_vector")
    pb.wait("scalar", "m_vector", pb.cnt["m_vector"])
    pb.op("scalar", lambda eng: eng.activation(st[:, 24:28], st[:, 20:24], AF.Sqrt,
                                               bias=S["epsc"][:, 1:2], scale=1.0), inc="m_scalar")
    pb.wait("vector", "m_scalar", pb.cnt["m_scalar"])
    pb.op("vector", lambda eng: eng.reciprocal(st[:, 24:28], st[:, 24:28]),
          inc="m_vector")
    pb.wait("vector", "m_vector", pb.cnt["m_vector"])
    pb.wait("vector", "q_gpsimd", S.get("_xntm_spill", 0))
    for i in range(4):
        gt = c * 4 + i
        xt = S["xld"][:, i * D:(i + 1) * D]
        xn = S["xn_tm"][:, i * D:(i + 1) * D]
        pb.op("vector", lambda eng, o=xn, i_=xt, m=st[:, 8 + i:9 + i], s_=st[:, 16 + i:17 + i]:
              eng.tensor_scalar(o, i_, m, s_, ALU.subtract, ALU.mult), inc="m_vector")
        S["_xld_busy"][i] = pb.cnt["m_vector"]
        pb.op("vector", lambda eng, o=S["xn2bf"][:, i * D:(i + 1) * D], i_=xn,
              r=st[:, 24 + i:25 + i]:
              eng.tensor_scalar(o, i_, r, None, ALU.mult), inc="m_vector")
        pb.wait("scalar", "m_vector", pb.cnt["m_vector"] - 1)
        pb.op("scalar", lambda eng, o=S["xnbf"][:, i * D:(i + 1) * D], i_=xn:
              eng.copy(o, i_), inc="m_scalar")
    vready = pb.cnt["m_vector"]
    sready = pb.cnt["m_scalar"]
    pb.wait("tensor", "m_vector", vready)
    pb.wait("tensor", "m_scalar", sready)
    items = []
    for i in range(4):
        for kt in range(KD):
            items.append((S["xnbf"][:, i * D + kt * 128: i * D + (kt + 1) * 128],
                          S["xn_fm"][:, kt * TC + i * 128: kt * TC + (i + 1) * 128]))
            items.append((S["xn2bf"][:, i * D + kt * 128: i * D + (kt + 1) * 128],
                          S["xn2_fm"][:, kt * TC + i * 128: kt * TC + (i + 1) * 128]))
    psts = S["_pstsB"]
    for jj, (inap, outap) in enumerate(items):
        slot = jj % 2
        pst = (S["psTb0"] if slot == 0 else S["psTb1"])[:]
        if psts[slot]:
            pb.wait("tensor", "m_scalar", psts[slot])
        pb.op("tensor", lambda eng, o=pst, i=inap, ey=S["eyeb"]:
              eng.transpose(o, i, ey[:]), inc="m_tensor")
        pb.wait("scalar", "m_tensor", pb.cnt["m_tensor"])
        pb.op("scalar", lambda eng, o=outap, i=pst: eng.copy(o, i),
              inc="m_scalar")
        psts[slot] = pb.cnt["m_scalar"]


def emit_l0_chunk_B(pb, al, S, s_scr, b, c):
    al.push()
    plA = al.sb("plA", [128, 8 * TC], F32)
    plB = al.sb("plB", [128, 8 * TC], F32)
    tmp = al.sb("scn_tmp", [128, TC], F32)
    nrm = al.sb("nrm", [128, TC], F32)
    S["upl"] = al.sb("upl", [128, 4 * TC], F32)
    S["hbf"] = al.sb("hbf", [128, 4 * TC], BF16)
    S["osout"] = al.sb("osout", [128, 2 * TC], F32)
    S["ostm"] = al.sb("ostm", [128, 4 * D], F32)
    pb.wait("tensor", "m_scalar", pb.cnt["m_scalar"])
    evac_cnt = S["_psAB_busy"]
    for ft in range(8):
        ps = S["psA"] if ft % 2 == 0 else S["psB"]
        if evac_cnt[ft % 2]:
            pb.wait("tensor", "m_vector", evac_cnt[ft % 2])
        for kt in range(KD):
            pb.op("tensor", lambda eng, o=ps, l=S["WmT"][:, kt * D + ft * 128: kt * D + (ft + 1) * 128],
                  r=S["xn2_fm"][:, kt * TC:(kt + 1) * TC], s0=(kt == 0), s1=(kt == KD - 1):
                  eng.matmul(o[:], l, r, start=s0, stop=s1))
        pb.mark("tensor", "m_tensor")
        pb.wait("vector", "m_tensor", pb.cnt["m_tensor"])
        pb.op("vector", lambda eng, o=plA[:, ft * TC:(ft + 1) * TC], i=ps:
              eng.tensor_copy(o, i[:]), inc="m_vector")
        evac_cnt[ft % 2] = pb.cnt["m_vector"]
    if PARANOID >= 2:
        pb.barrier()
    pb.dep("vector")
    for kkt in range(2):
        a = plA[:, (0 * 2 + kkt) * TC:(0 * 2 + kkt + 1) * TC]
        bpl = plA[:, (1 * 2 + kkt) * TC:(1 * 2 + kkt + 1) * TC]
        cpl = plA[:, (2 * 2 + kkt) * TC:(2 * 2 + kkt + 1) * TC]
        dpl = plA[:, (3 * 2 + kkt) * TC:(3 * 2 + kkt + 1) * TC]
        pb.op("vector", lambda eng, a=a: eng.tensor_tensor(nrm[:], a, a, ALU.mult))
        pb.op("vector", lambda eng, b_=bpl: eng.tensor_tensor(tmp[:], b_, b_, ALU.mult))
        pb.dep("vector")
        pb.op("vector", lambda eng: eng.tensor_tensor(nrm[:], nrm[:], tmp[:], ALU.add))
        pb.op("vector", lambda eng, c_=cpl: eng.tensor_tensor(tmp[:], c_, c_, ALU.mult))
        pb.dep("vector")
        pb.op("vector", lambda eng: eng.tensor_tensor(nrm[:], nrm[:], tmp[:], ALU.add))
        pb.op("vector", lambda eng, d_=dpl: eng.tensor_tensor(tmp[:], d_, d_, ALU.mult))
        pb.dep("vector")
        pb.op("vector", lambda eng: eng.tensor_tensor(nrm[:], nrm[:], tmp[:], ALU.add),
              inc="m_vector")
        pb.wait("scalar", "m_vector", pb.cnt["m_vector"])
        pb.op("scalar", lambda eng: eng.activation(nrm[:], nrm[:], AF.Sqrt,
                                                   bias=0.0, scale=1.0), inc="m_scalar")
        pb.wait("vector", "m_scalar", pb.cnt["m_scalar"])
        pb.op("vector", lambda eng: eng.tensor_scalar(nrm[:], nrm[:], 1e-6, None, ALU.max))
        pb.dep("vector")
        pb.op("vector", lambda eng: eng.reciprocal(nrm[:], nrm[:]))
        pb.dep("vector")
        for pl_ap in (a, bpl, cpl, dpl):
            pb.op("vector", lambda eng, p=pl_ap: eng.tensor_tensor(p, p, nrm[:], ALU.mult))
        pb.mark("vector", "m_vector")
    if S.get("_dump_pl") is not None and c == 0:
        pb.barrier()
        pb.dma("sync", S["_dump_pl"][:], plA[:])
        pb.barrier()
    pb.dep("vector")
    cur, oth = plA, plB
    s = 1
    while s < TC:
        for kkt in range(2):
            sl = lambda pl: cur[:, (pl * 2 + kkt) * TC:(pl * 2 + kkt + 1) * TC]
            dl = lambda pl: oth[:, (pl * 2 + kkt) * TC:(pl * 2 + kkt + 1) * TC]
            a_h, a_l = sl(0)[:, s:TC], sl(0)[:, 0:TC - s]
            b_h, b_l = sl(1)[:, s:TC], sl(1)[:, 0:TC - s]
            c_h, c_l = sl(2)[:, s:TC], sl(2)[:, 0:TC - s]
            d_h, d_l = sl(3)[:, s:TC], sl(3)[:, 0:TC - s]
            tm = tmp[:, s:TC]
            for (dst_pl, x_h, y_l, z_h, w_l) in (
                    (0, a_h, a_l, b_h, c_l),
                    (1, a_h, b_l, b_h, d_l),
                    (2, c_h, a_l, d_h, c_l),
                    (3, c_h, b_l, d_h, d_l)):
                dd = dl(dst_pl)[:, s:TC]
                pb.op("vector", lambda eng, o=dd, i0=x_h, i1=y_l:
                      eng.tensor_tensor(o, i0, i1, ALU.mult))
                pb.op("vector", lambda eng, o=tm, i0=z_h, i1=w_l:
                      eng.tensor_tensor(o, i0, i1, ALU.mult))
                pb.dep("vector")
                pb.op("vector", lambda eng, o=dd, i1=tm:
                      eng.tensor_tensor(o, o, i1, ALU.add))
                pb.op("vector", lambda eng, o=dl(dst_pl)[:, 0:s], i=sl(dst_pl)[:, 0:s]:
                      eng.tensor_copy(o, i))
                pb.dep("vector")
        if PARANOID >= 1:
            pb.barrier()
        pb.mark("vector", "m_vector")
        pb.mark("scalar", "m_scalar")
        pb.wait("vector", "m_scalar", pb.cnt["m_scalar"])
        pb.wait("scalar", "m_vector", pb.cnt["m_vector"])
        pb.dep("vector")
        cur, oth = oth, cur
        s *= 2
    if PARANOID >= 2:
        pb.barrier()
    if S.get("_dump_plf") is not None and c == 0:
        pb.barrier()
        pb.dma("sync", S["_dump_plf"][:], cur[:])
        pb.barrier()
    carry = S["carry"]
    upl = S["upl"]
    if S.get("_dump_carry") is not None and c == S.get("_dump_chunk", 0):
        pb.barrier()
        pb.dma("sync", S["_dump_carry"][:], S["carry"][:])
        pb.barrier()
    pb.dep("vector")
    for kkt in range(2):
        a = cur[:, (0 * 2 + kkt) * TC:(0 * 2 + kkt + 1) * TC]
        bpl = cur[:, (1 * 2 + kkt) * TC:(1 * 2 + kkt + 1) * TC]
        cpl = cur[:, (2 * 2 + kkt) * TC:(2 * 2 + kkt + 1) * TC]
        dpl = cur[:, (3 * 2 + kkt) * TC:(3 * 2 + kkt + 1) * TC]
        u0 = upl[:, (0 * 2 + kkt) * TC:(0 * 2 + kkt + 1) * TC]
        u1 = upl[:, (1 * 2 + kkt) * TC:(1 * 2 + kkt + 1) * TC]
        pb.op("vector", lambda eng, o=u0, i=a, s1=carry[:, kkt:kkt + 1]:
              eng.tensor_scalar(o, i, s1, None, ALU.mult))
        pb.op("vector", lambda eng, o=tmp[:], i=bpl, s1=carry[:, 2 + kkt:3 + kkt]:
              eng.tensor_scalar(o, i, s1, None, ALU.mult))
        pb.dep("vector")
        pb.op("vector", lambda eng, o=u0, i1=tmp[:]:
              eng.tensor_tensor(o, o, i1, ALU.add))
        pb.op("vector", lambda eng, o=u1, i=cpl, s1=carry[:, kkt:kkt + 1]:
              eng.tensor_scalar(o, i, s1, None, ALU.mult))
        pb.op("vector", lambda eng, o=tmp[:], i=dpl, s1=carry[:, 2 + kkt:3 + kkt]:
              eng.tensor_scalar(o, i, s1, None, ALU.mult))
        pb.dep("vector")
        pb.op("vector", lambda eng, o=u1, i1=tmp[:]:
              eng.tensor_tensor(o, o, i1, ALU.add))
        pb.dep("vector")
        pb.op("vector", lambda eng, o=carry[:, kkt:kkt + 1], i=u0[:, TC - 1:TC]:
              eng.tensor_copy(o, i))
        pb.op("vector", lambda eng, o=carry[:, 2 + kkt:3 + kkt], i=u1[:, TC - 1:TC]:
              eng.tensor_copy(o, i), inc="m_vector")
    if S.get("_dump_upl") is not None and c == S.get("_dump_chunk", 0):
        pb.barrier()
        pb.dma("sync", S["_dump_upl"][:], upl[:])
        pb.barrier()
    pb.wait("scalar", "m_vector", pb.cnt["m_vector"])
    for jt in range(4):
        pb.op("scalar", lambda eng, o=S["hbf"][:, jt * TC:(jt + 1) * TC],
              i=upl[:, jt * TC:(jt + 1) * TC]: eng.copy(o, i), inc="m_scalar")
    if PARANOID >= 4:
        pb.barrier()
    pb.wait("tensor", "m_scalar", pb.cnt["m_scalar"])
    evac_cnt = S["_psAB_busy"]
    tev = S["_psT_busy"]
    osread = S["_osout_busy"]
    for ft in range(8):
        ps = S["psA"] if ft % 2 == 0 else S["psB"]
        if evac_cnt[ft % 2]:
            pb.wait("tensor", "m_vector", evac_cnt[ft % 2])
        for jt in range(4):
            pb.op("tensor", lambda eng, o=ps,
                  l=S["WpT"][:, jt * D + ft * 128: jt * D + (ft + 1) * 128],
                  r=S["hbf"][:, jt * TC:(jt + 1) * TC], s0=(jt == 0), s1=(jt == 3):
                  eng.matmul(o[:], l, r, start=s0, stop=s1))
        pb.mark("tensor", "m_tensor")
        pb.wait("vector", "m_tensor", pb.cnt["m_tensor"])
        if osread[ft % 2]:
            pb.wait("vector", "m_tensor", osread[ft % 2])
        pb.op("vector", lambda eng, o=S["osout"][:, (ft % 2) * TC:(ft % 2 + 1) * TC], i=ps,
              s1=S["osc"][:, ft:ft + 1]:
              eng.tensor_scalar(o, i[:], s1, None, ALU.mult), inc="m_vector")
        evac_cnt[ft % 2] = pb.cnt["m_vector"]
        for tt in range(4):
            slot = tt % 2
            pst = (S["psTf0"] if slot == 0 else S["psTf1"])[:]
            if tev[slot]:
                pb.wait("tensor", "m_scalar", tev[slot])
            pb.wait("tensor", "m_vector", evac_cnt[ft % 2])
            pb.op("tensor", lambda eng, o=pst,
                  i=S["osout"][:, (ft % 2) * TC + tt * 128:(ft % 2) * TC + (tt + 1) * 128],
                  ey=S["eyef"]: eng.transpose(o, i, ey[:]), inc="m_tensor")
            pb.wait("scalar", "m_tensor", pb.cnt["m_tensor"])
            pb.op("scalar", lambda eng, o=S["ostm"][:, tt * D + ft * 128: tt * D + (ft + 1) * 128],
                  i=pst: eng.copy(o, i), inc="m_scalar")
            tev[slot] = pb.cnt["m_scalar"]
        osread[ft % 2] = pb.cnt["m_tensor"]
    if S.get("_dump_ostm") is not None and c == S.get("_dump_chunk", 0):
        pb.barrier()
        pb.dma("sync", S["_dump_ostm"][:], S["ostm"][:])
        pb.barrier()
    pb.wait("vector", "m_scalar", pb.cnt["m_scalar"])
    for tt in range(4):
        gt = c * 4 + tt
        pb.op("vector", lambda eng, o=S["xn_tm"][:, tt * D:(tt + 1) * D],
              i1=S["ostm"][:, tt * D:(tt + 1) * D]:
              eng.tensor_tensor(o, o, i1, ALU.add), inc="m_vector")
        pb.wait("gpsimd", "m_vector", pb.cnt["m_vector"])
        pb.dma("gpsimd", s_scr[gt * 128:(gt + 1) * 128, :],
               S["xn_tm"][:, tt * D:(tt + 1) * D])
        S["_xntm_spill"] = pb.cnt["q_gpsimd"]
    pb.barrier()
    al.pop()


def emit_l0_chunk_MOE(pb, al, S, W, b, c, part=3):
    al.push()
    if part < -1.5:
        al.pop()
        return
    guW = al.sb("guW", [128, KD * 2048], BF16)
    dpW = al.sb("dpW", [128, KD * D], BF16)
    wbc = al.sb("wbc", [128, 8 * TC], F32)
    hid = al.sb("hid", [128, KD * TC], BF16)
    acc = S["acc"]
    sgate = al.sb("sgate", [128, TC], F32)
    t1 = al.sb("moet1", [128, TC], F32)
    mst = al.sb("mst", [128, 4 * 8], F32)
    msc = al.sb("msc", [128, 32], F32)
    if part >= -0.7:
        cfm = al.sb("cfm", [8, TC], F32)
        wrow = al.sb("wrow", [1, TC], F32)
    if part < -0.5:
        al.pop()
        return

    pb.op("vector", lambda eng: eng.memset(acc[:], 0.0), inc="m_vector")
    if part < 0.4:
        pb.barrier()
        al.pop()
        return
    for i in range(4):
        ps = S["psC"]
        if i >= 1:
            pb.wait("tensor", "m_vector", pb.cnt["m_vector"])
        for kt in range(KD):
            pb.op("tensor", lambda eng, o=ps, l=S["xn_fm"][:, kt * TC + i * 128: kt * TC + (i + 1) * 128],
                  r=S["rtT"][:, kt * 8:(kt + 1) * 8], s0=(kt == 0), s1=(kt == KD - 1):
                  eng.matmul(o[:, 0:8], l, r, start=s0, stop=s1))
        pb.mark("tensor", "m_tensor")
        pb.wait("vector", "m_tensor", pb.cnt["m_tensor"])
        pb.op("vector", lambda eng, o=mst[:, i * 8:(i + 1) * 8], i_=ps:
              eng.tensor_copy(o, i_[:, 0:8]), inc="m_vector")
    if part < 0.8:
        pb.barrier()
        al.pop()
        return
    pb.dep("vector")
    for i in range(4):
        pb.op("vector", lambda eng, o=msc[:, 0 + i:1 + i], i_=mst[:, i * 8:(i + 1) * 8]:
              eng.reduce_max(o, i_, axis=AX, negate=True))
    pb.mark("vector", "m_vector")
    pb.wait("scalar", "m_vector", pb.cnt["m_vector"])
    for i in range(4):
        pb.op("scalar", lambda eng, o=mst[:, i * 8:(i + 1) * 8], i_=mst[:, i * 8:(i + 1) * 8],
              bi=msc[:, 0 + i:1 + i], a=msc[:, 4 + i:5 + i]:
              eng.activation(o, i_, AF.Exp, bias=bi, scale=1.0, accum_out=a))
    pb.mark("scalar", "m_scalar")
    pb.wait("vector", "m_scalar", pb.cnt["m_scalar"])
    pb.op("vector", lambda eng: eng.reciprocal(msc[:, 4:8], msc[:, 4:8]))
    pb.dep("vector")
    for i in range(4):
        pb.op("vector", lambda eng, o=mst[:, i * 8:(i + 1) * 8], i_=mst[:, i * 8:(i + 1) * 8],
              s1=msc[:, 4 + i:5 + i]:
              eng.tensor_scalar(o, i_, s1, None, ALU.mult))
    if part < 0.9:
        pb.barrier()
        al.pop()
        return
    pb.dep("vector")
    for i in range(4):
        pb.op("vector", lambda eng, o=msc[:, 8 + i:9 + i], i_=mst[:, i * 8:(i + 1) * 8]:
              eng.reduce_max(o, i_, axis=AX))
    pb.dep("vector")
    eq = al.sb("moeeq", [128, 4 * 8 * 2], F32)
    for i in range(4):
        pb.op("vector", lambda eng, o=eq[:, i * 8:(i + 1) * 8], i_=mst[:, i * 8:(i + 1) * 8],
              s1=msc[:, 8 + i:9 + i]:
              eng.tensor_single_scalar(o, i_, s1, ALU.is_equal))
    pb.dep("vector")
    for i in range(4):
        pb.op("vector", lambda eng, o=mst[:, i * 8:(i + 1) * 8], i0=eq[:, i * 8:(i + 1) * 8],
              i1=mst[:, i * 8:(i + 1) * 8]:
              eng.scalar_tensor_tensor(o, i0, -1e30, i1, ALU.mult, ALU.add))
    pb.dep("vector")
    for i in range(4):
        pb.op("vector", lambda eng, o=msc[:, 12 + i:13 + i], i_=mst[:, i * 8:(i + 1) * 8]:
              eng.reduce_max(o, i_, axis=AX))
    pb.dep("vector")
    for i in range(4):
        pb.op("vector", lambda eng, o=eq[:, 32 + i * 8:32 + (i + 1) * 8],
              i_=mst[:, i * 8:(i + 1) * 8], s1=msc[:, 12 + i:13 + i]:
              eng.tensor_single_scalar(o, i_, s1, ALU.is_equal))
    pb.op("vector", lambda eng: eng.tensor_tensor(msc[:, 16:20], msc[:, 8:12], msc[:, 12:16], ALU.add))
    pb.dep("vector")
    pb.op("vector", lambda eng: eng.reciprocal(msc[:, 16:20], msc[:, 16:20]))
    pb.dep("vector")
    pb.op("vector", lambda eng: eng.tensor_tensor(msc[:, 20:24], msc[:, 12:16], msc[:, 16:20], ALU.mult))
    pb.op("vector", lambda eng: eng.tensor_tensor(msc[:, 16:20], msc[:, 8:12], msc[:, 16:20], ALU.mult))
    pb.dep("vector")
    for i in range(4):
        pb.op("vector", lambda eng, o=mst[:, i * 8:(i + 1) * 8], i_=eq[:, i * 8:(i + 1) * 8],
              s1=msc[:, 16 + i:17 + i]:
              eng.tensor_scalar(o, i_, s1, None, ALU.mult))
    pb.dep("vector")
    for i in range(4):
        pb.op("vector", lambda eng, o=mst[:, i * 8:(i + 1) * 8], i0=eq[:, 32 + i * 8:32 + (i + 1) * 8],
              s1=msc[:, 20 + i:21 + i], i1=mst[:, i * 8:(i + 1) * 8]:
              eng.scalar_tensor_tensor(o, i0, s1, i1, ALU.mult, ALU.add), inc="m_vector")
    if part < 2:
        pb.barrier()
        al.pop()
        return
    pb.wait("tensor", "m_vector", pb.cnt["m_vector"])
    for i in range(4):
        pb.op("tensor", lambda eng, o=S["psTf0"][0:8, :], i_=mst[:, i * 8:(i + 1) * 8]:
              eng.transpose(o, i_, S["eyef"][:]), inc="m_tensor")
        pb.wait("scalar", "m_tensor", pb.cnt["m_tensor"])
        pb.op("scalar", lambda eng, o=cfm[0:8, i * 128:(i + 1) * 128], i_=S["psTf0"][0:8, :]:
              eng.copy(o, i_), inc="m_scalar")
        pb.wait("tensor", "m_scalar", pb.cnt["m_scalar"])
    for e in range(8):
        pb.wait("gpsimd", "m_scalar", pb.cnt["m_scalar"])
        if e >= 1:
            pb.wait("gpsimd", "m_tensor", pb.cnt["m_tensor"])
        cdma = pb.dma("gpsimd", wrow[0:1, :], cfm[e:e + 1, :])
        pb.wait("tensor", "q_gpsimd", cdma)
        if e >= 1:
            pb.wait("tensor", "m_vector", pb.cnt["m_vector"])
        pb.op("tensor", lambda eng, o=S["psD"]:
              eng.matmul(o[:], S["onesf"][0:1, :], wrow[0:1, :], start=True, stop=True),
              inc="m_tensor")
        pb.wait("vector", "m_tensor", pb.cnt["m_tensor"])
        pb.op("vector", lambda eng, o=wbc[:, e * TC:(e + 1) * TC], i_=S["psD"]:
              eng.tensor_copy(o, i_[:]), inc="m_vector")
    if part < 3:
        pb.barrier()
        al.pop()
        return
    if S.get("_dump_comb") is not None and c == S.get("_dump_chunk", 0) and b == 0:
        pb.barrier()
        pb.dma("sync", S["_dump_comb"][:], mst[:])
        pb.barrier()
    guT3 = W["guT_bf"].rearrange("(e k p) f -> (e k) p f", p=128, e=8)
    dpT3 = W["dpT_bf"].rearrange("(e k p) f -> (e k) p f", p=128, e=8)
    gu_busy = 0
    dp_busy = 0
    for e in range(8):
        if gu_busy:
            pb.wait("sync", "m_tensor", gu_busy)
        gl = 0
        for kt in range(KD):
            gl = pb.dma("sync", guW[:, kt * 2048:(kt + 1) * 2048], guT3[e * KD + kt])
        for kt in range(KD):
            dl = pb.dma("sync", dpW[:, kt * D:(kt + 1) * D], dpT3[e * KD + kt])
        pb.wait("tensor", "q_sync", pb.cnt["q_sync"])
        ev = S["_psCD_busy"]
        for f16 in range(8):
            psg = S["psC"]
            psu = S["psD"]
            if ev[0]:
                pb.wait("tensor", "m_scalar", ev[0])
            for kt in range(KD):
                pb.op("tensor", lambda eng, o=psg, l=guW[:, kt * 2048 + f16 * 128: kt * 2048 + (f16 + 1) * 128],
                      r=S["xn_fm"][:, kt * TC:(kt + 1) * TC], s0=(kt == 0), s1=(kt == KD - 1):
                      eng.matmul(o[:], l, r, start=s0, stop=s1))
            pb.mark("tensor", "m_tensor")
            gate_done = pb.cnt["m_tensor"]
            if ev[1]:
                pb.wait("tensor", "m_vector", ev[1])
            for kt in range(KD):
                pb.op("tensor", lambda eng, o=psu, l=guW[:, kt * 2048 + (8 + f16) * 128: kt * 2048 + (9 + f16) * 128],
                      r=S["xn_fm"][:, kt * TC:(kt + 1) * TC], s0=(kt == 0), s1=(kt == KD - 1):
                      eng.matmul(o[:], l, r, start=s0, stop=s1))
            pb.mark("tensor", "m_tensor")
            up_done = pb.cnt["m_tensor"]
            pb.wait("scalar", "m_tensor", gate_done)
            pb.op("scalar", lambda eng, o=sgate[:], i_=psg:
                  eng.activation(o, i_[:], AF.Silu, bias=0.0, scale=1.0), inc="m_scalar")
            ev[0] = pb.cnt["m_scalar"]
            pb.wait("vector", "m_scalar", ev[0])
            pb.wait("vector", "m_tensor", up_done)
            pb.op("vector", lambda eng, o=t1[:], i0=psu, i1=sgate[:]:
                  eng.tensor_tensor(o, i0[:], i1, ALU.mult))
            pb.dep("vector")
            pb.op("vector", lambda eng, o=hid[:, f16 * TC:(f16 + 1) * TC], i0=t1[:],
                  i1=wbc[:, e * TC:(e + 1) * TC]:
                  eng.tensor_tensor(o, i0, i1, ALU.mult), inc="m_vector")
            ev[1] = pb.cnt["m_vector"]
        pb.wait("tensor", "m_vector", pb.cnt["m_vector"])
        for fo in range(8):
            psd = S["psC"] if fo % 2 == 0 else S["psD"]
            if S["_psDn_busy"][fo % 2]:
                pb.wait("tensor", "m_vector", S["_psDn_busy"][fo % 2])
            for kt in range(KD):
                pb.op("tensor", lambda eng, o=psd, l=dpW[:, kt * D + fo * 128: kt * D + (fo + 1) * 128],
                      r=hid[:, kt * TC:(kt + 1) * TC], s0=(kt == 0), s1=(kt == KD - 1):
                      eng.matmul(o[:], l, r, start=s0, stop=s1))
            pb.mark("tensor", "m_tensor")
            pb.wait("vector", "m_tensor", pb.cnt["m_tensor"])
            pb.op("vector", lambda eng, o=acc[:, fo * TC:(fo + 1) * TC], i1=psd:
                  eng.tensor_tensor(o, o, i1[:], ALU.add), inc="m_vector")
            S["_psDn_busy"][fo % 2] = pb.cnt["m_vector"]
        gu_busy = pb.cnt["m_tensor"]
    if S.get("_dump_acc") is not None and c == S.get("_dump_chunk", 0) and b == 0:
        pb.barrier()
        pb.dma("sync", S["_dump_acc"][:], acc[:])
        pb.barrier()
    pb.barrier()
    al.pop()


def emit_l0_chunk_COMB(pb, al, S, xap, outap, W, b, c):
    al.push()
    gsil = al.sb("gsil", [128, 128], F32)
    gtm = al.sb("gtm", [128, 2 * D], F32)
    xre = al.sb("xre", [128, 2 * D], F32)
    sre = al.sb("sre", [128, 2 * D], F32)
    sgt = al.sb("sgt", [128, D], F32)
    acc = S["acc"]
    tev = [0, 0]
    for tt in range(4):
        gt = c * 4 + tt
        sl = tt % 2
        xr = xre[:, sl * D:(sl + 1) * D]
        sr = sre[:, sl * D:(sl + 1) * D]
        if S["_comb_ld_busy"][sl]:
            pb.wait("sync", "m_vector", S["_comb_ld_busy"][sl])
        if S["_comb_st_busy"][sl]:
            pb.wait("sync", "q_gpsimd", S["_comb_st_busy"][sl])
        cx = pb.dma("sync", xr, xap[b, gt * 128:(gt + 1) * 128, :])
        cs = pb.dma("sync", sr, W["s_scr"][gt * 128:(gt + 1) * 128, :])
        first_evac = True
        for fo in range(8):
            if S["_gsil_busy"]:
                pb.wait("scalar", "m_tensor", S["_gsil_busy"])
            pb.op("scalar", lambda eng, o=gsil[:], i_=acc[:, fo * TC + tt * 128: fo * TC + (tt + 1) * 128]:
                  eng.activation(o, i_, AF.Silu, bias=0.0, scale=1.0), inc="m_scalar")
            pb.wait("tensor", "m_scalar", pb.cnt["m_scalar"])
            pst = S["psTf0"][:]
            if tev[1]:
                pb.wait("tensor", "m_scalar", tev[1])
            pb.op("tensor", lambda eng, o=pst, i_=gsil[:]:
                  eng.transpose(o, i_, S["eyef"][:]), inc="m_tensor")
            S["_gsil_busy"] = pb.cnt["m_tensor"]
            pb.wait("scalar", "m_tensor", pb.cnt["m_tensor"])
            if first_evac and S["_gtm_busy"][sl]:
                pb.wait("scalar", "m_vector", S["_gtm_busy"][sl])
            first_evac = False
            pb.op("scalar", lambda eng, o=gtm[:, sl * D + fo * 128: sl * D + (fo + 1) * 128],
                  i_=pst: eng.copy(o, i_), inc="m_scalar")
            tev[1] = pb.cnt["m_scalar"]
        pb.wait("vector", "m_scalar", pb.cnt["m_scalar"])
        pb.wait("vector", "q_sync", cs)
        pb.op("vector", lambda eng, o=sgt[:], i0=sr, i1=gtm[:, sl * D:(sl + 1) * D]:
              eng.tensor_tensor(o, i0, i1, ALU.mult), inc="m_vector")
        S["_gtm_busy"][sl] = pb.cnt["m_vector"]
        pb.dep("vector")
        pb.wait("vector", "q_sync", cx)
        pb.op("vector", lambda eng, o=xr, i0=xr, i1=sgt[:]:
              eng.tensor_tensor(o, i0, i1, ALU.add), inc="m_vector")
        S["_comb_ld_busy"][sl] = pb.cnt["m_vector"]
        pb.wait("gpsimd", "m_vector", pb.cnt["m_vector"])
        pb.dma("gpsimd", outap[b, gt * 128:(gt + 1) * 128, :], xr)
        S["_comb_st_busy"][sl] = pb.cnt["q_gpsimd"]
    pb.barrier()
    al.pop()



def emit_l1_A(pb, S, x1ap, W, b):
    st = S["stat"]
    for c in range(NCH):
        for i in range(4):
            gt = c * 4 + i
            xt = S["xld"][:, i * D:(i + 1) * D]
            pb.wait("sync", "m_vector", S["_xld_busy"][i])
            cload = pb.dma("sync", xt, x1ap[b, gt * 128:(gt + 1) * 128, :])
            pb.wait("vector", "q_sync", cload)
            pb.op("vector", lambda eng, o=st[:, 0 + i:1 + i], i_=xt:
                  eng.reduce_sum(o, i_, axis=AX), inc="m_vector")
            pb.wait("scalar", "q_sync", cload)
            pb.op("scalar", lambda eng, o=S["sqdump"][:], i_=xt, a=st[:, 4 + i:5 + i]:
                  eng.activation(o, i_, AF.Square, bias=0.0, scale=1.0, accum_out=a),
                  inc="m_scalar")
        pb.wait("vector", "m_scalar", pb.cnt["m_scalar"])
        pb.dep("vector")
        pb.op("vector", lambda eng: eng.tensor_scalar(st[:, 8:12], st[:, 0:4], 1.0 / D, None, ALU.mult))
        pb.dep("vector")
        pb.op("vector", lambda eng: eng.tensor_tensor(st[:, 12:16], st[:, 8:12], st[:, 8:12], ALU.mult))
        pb.op("vector", lambda eng: eng.tensor_scalar(st[:, 0:4], st[:, 4:8], 1.0 / D, None, ALU.mult))
        pb.dep("vector")
        pb.op("vector", lambda eng: eng.tensor_tensor(st[:, 12:16], st[:, 0:4], st[:, 12:16], ALU.subtract),
              inc="m_vector")
        pb.wait("scalar", "m_vector", pb.cnt["m_vector"])
        pb.op("scalar", lambda eng: eng.activation(st[:, 16:20], st[:, 12:16], AF.Sqrt,
                                                   bias=S["epsc"][:, 0:1], scale=1.0), inc="m_scalar")
        pb.wait("vector", "m_scalar", pb.cnt["m_scalar"])
        pb.op("vector", lambda eng: eng.reciprocal(st[:, 16:20], st[:, 16:20]), inc="m_vector")
        pb.wait("vector", "m_vector", pb.cnt["m_vector"])
        pb.wait("vector", "q_gpsimd", S.get("_xn1_spill", 0))
        for i in range(4):
            gt = c * 4 + i
            xt = S["xld"][:, i * D:(i + 1) * D]
            xn = S["xn_tm"][:, i * D:(i + 1) * D]
            pb.op("vector", lambda eng, o=xn, i_=xt, m=st[:, 8 + i:9 + i], s_=st[:, 16 + i:17 + i]:
                  eng.tensor_scalar(o, i_, m, s_, ALU.subtract, ALU.mult), inc="m_vector")
            S["_xld_busy"][i] = pb.cnt["m_vector"]
            pb.wait("gpsimd", "m_vector", pb.cnt["m_vector"])
            pb.dma("gpsimd", W["xn1_scr"][gt * 128:(gt + 1) * 128, :], xn)
            S["_xn1_spill"] = pb.cnt["q_gpsimd"]
            pb.wait("scalar", "m_vector", pb.cnt["m_vector"])
            pb.op("scalar", lambda eng, o=S["xnbf"][:, i * D:(i + 1) * D], i_=xn:
                  eng.copy(o, i_), inc="m_scalar")
        vready = pb.cnt["m_vector"]
        sready = pb.cnt["m_scalar"]
        pb.wait("tensor", "m_vector", vready)
        pb.wait("tensor", "m_scalar", sready)
        items = []
        for i in range(4):
            for kt in range(KD):
                items.append((S["xnbf"][:, i * D + kt * 128: i * D + (kt + 1) * 128],
                              S["xn1_fm"][:, kt * T + c * TC + i * 128: kt * T + c * TC + (i + 1) * 128]))
        psts = S["_pstsB"]
        for jj, (inap, outap) in enumerate(items):
            slot = jj % 2
            pst = (S["psTb0"] if slot == 0 else S["psTb1"])[:]
            if psts[slot]:
                pb.wait("tensor", "m_scalar", psts[slot])
            pb.op("tensor", lambda eng, o=pst, i=inap, ey=S["eyeb"]:
                  eng.transpose(o, i, ey[:]), inc="m_tensor")
            pb.wait("scalar", "m_tensor", pb.cnt["m_tensor"])
            pb.op("scalar", lambda eng, o=outap, i=pst: eng.copy(o, i),
                  inc="m_scalar")
            psts[slot] = pb.cnt["m_scalar"]


def emit_l1_QKV(pb, al, S, W, b):
    al.push()
    psq = [al.ps("psq", [128, TC], F32) for _ in range(4)]
    psTb0 = al.ps("psTbq0", [128, 128], BF16)
    psTb1 = al.ps("psTbq1", [128, 128], BF16)
    wring = al.sb("wqring", [128, 2 * KD * 128], BF16)
    vtmp = al.sb("vtmp", [128, TC], BF16)
    winT3 = W["winT_bf"].rearrange("(k p) f -> k p f", p=128)
    wbusy = [0, 0]
    psbusy = [0, 0, 0, 0]
    vt_busy = 0
    tb_busy = [0, 0]
    for f24 in range(24):
        sl = f24 % 2
        wr = wring[:, sl * KD * 128:(sl + 1) * KD * 128]
        if wbusy[sl]:
            pb.wait("sync", "m_tensor", wbusy[sl])
        cw = pb.dma("sync", wr.rearrange("p (k f) -> p k f", k=KD),
                    winT3[:, :, f24 * 128:(f24 + 1) * 128].rearrange("k p f -> p k f"))
        pb.wait("tensor", "q_sync", cw)
        for ch in range(NCH):
            if psbusy[ch]:
                pb.wait("tensor", "m_scalar", psbusy[ch])
            for kt in range(KD):
                pb.op("tensor", lambda eng, o=psq[ch], l=wr[:, kt * 128:(kt + 1) * 128],
                      r=S["xn1_fm"][:, kt * T + ch * TC:(kt) * T + (ch + 1) * TC],
                      s0=(kt == 0), s1=(kt == KD - 1):
                      eng.matmul(o[:], l, r, start=s0, stop=s1))
        pb.mark("tensor", "m_tensor")
        wbusy[sl] = pb.cnt["m_tensor"]
        mmdone = pb.cnt["m_tensor"]
        if f24 < 16:
            dst = S["q_bf"] if f24 < 8 else S["k_bf"]
            ft = f24 % 8
            pb.wait("scalar", "m_tensor", mmdone)
            for ch in range(NCH):
                pb.op("scalar", lambda eng, o=dst[:, ft * T + ch * TC: ft * T + (ch + 1) * TC],
                      i_=psq[ch], bi=S["binS"][:, f24:f24 + 1]:
                      eng.activation(o, i_[:], AF.Identity, bias=bi, scale=1.0), inc="m_scalar")
                psbusy[ch] = pb.cnt["m_scalar"]
        else:
            h2 = f24 - 16
            for ch in range(NCH):
                pb.wait("scalar", "m_tensor", mmdone)
                if vt_busy:
                    pb.wait("scalar", "m_tensor", vt_busy)
                pb.op("scalar", lambda eng, o=vtmp[:], i_=psq[ch], bi=S["binS"][:, f24:f24 + 1]:
                      eng.activation(o, i_[:], AF.Identity, bias=bi, scale=1.0), inc="m_scalar")
                psbusy[ch] = pb.cnt["m_scalar"]
                pb.wait("tensor", "m_scalar", pb.cnt["m_scalar"])
                for j in range(4):
                    tsl = j % 2
                    pst = (psTb0 if tsl == 0 else psTb1)[:]
                    if tb_busy[tsl]:
                        pb.wait("tensor", "m_scalar", tb_busy[tsl])
                    pb.op("tensor", lambda eng, o=pst, i_=vtmp[:, j * 128:(j + 1) * 128],
                          ey=S["eyeb"]: eng.transpose(o, i_, ey[:]), inc="m_tensor")
                    pb.wait("scalar", "m_tensor", pb.cnt["m_tensor"])
                    tb = ch * 4 + j
                    pb.op("scalar", lambda eng, o=S["vT"][:, tb * D + h2 * 128:(tb) * D + (h2 + 1) * 128],
                          i_=pst: eng.copy(o, i_), inc="m_scalar")
                    tb_busy[tsl] = pb.cnt["m_scalar"]
                vt_busy = pb.cnt["m_tensor"]
    pb.barrier()
    al.pop()


def emit_l1_FFN(pb, al, S, W, b):
    al.push()
    psg = al.ps("psg", [128, TC], F32)
    psu = al.ps("psu", [128, TC], F32)
    psd = al.ps("psd", [128, TC], F32)
    psTf0 = al.ps("psTff0", [128, 128], F32)
    psTf1 = al.ps("psTff1", [128, 128], F32)
    wgr = al.sb("wgring", [128, 2 * KD * 128], BF16)
    wur = al.sb("wuring", [128, 2 * KD * 128], BF16)
    wdr = al.sb("wdring", [128, 24 * 128], BF16)
    sgate = al.sb("sgate1", [128, TC], F32)
    hid = al.sb("hid1", [128, 24 * TC], BF16)
    gf = al.sb("gf", [128, TC], F32)
    gtm = al.sb("gtm1", [128, 4 * D], F32)
    wgT3 = W["wgT_bf"].rearrange("(k p) f -> k p f", p=128)
    wuT3 = W["wuT_bf"].rearrange("(k p) f -> k p f", p=128)
    wdT3 = W["wdT_bf"].rearrange("(k p) f -> k p f", p=128)
    INV = 1.0 / 127.0
    wbusy = [0, 0]
    wdbusy = [0, 0]
    hid_busy = 0
    sg_busy = 0
    gf_busy = [0, 0]
    for ch in range(NCH):
        for f16 in range(24):
            sl = f16 % 2
            wg = wgr[:, sl * KD * 128:(sl + 1) * KD * 128]
            wu = wur[:, sl * KD * 128:(sl + 1) * KD * 128]
            if wbusy[sl]:
                pb.wait("sync", "m_tensor", wbusy[sl])
            cw = pb.dma("sync", wg.rearrange("p (k f) -> p k f", k=KD),
                        wgT3[:, :, f16 * 128:(f16 + 1) * 128].rearrange("k p f -> p k f"))
            cw2 = pb.dma("sync", wu.rearrange("p (k f) -> p k f", k=KD),
                         wuT3[:, :, f16 * 128:(f16 + 1) * 128].rearrange("k p f -> p k f"))
            pb.wait("tensor", "q_sync", cw2)
            if sg_busy:
                pb.wait("tensor", "m_vector", sg_busy)
            for kt in range(KD):
                pb.op("tensor", lambda eng, o=psg, l=wg[:, kt * 128:(kt + 1) * 128],
                      r=S["xn1_fm"][:, kt * T + ch * TC: kt * T + (ch + 1) * TC],
                      s0=(kt == 0), s1=(kt == KD - 1):
                      eng.matmul(o[:], l, r, start=s0, stop=s1))
            pb.mark("tensor", "m_tensor")
            g_done = pb.cnt["m_tensor"]
            for kt in range(KD):
                pb.op("tensor", lambda eng, o=psu, l=wu[:, kt * 128:(kt + 1) * 128],
                      r=S["xn1_fm"][:, kt * T + ch * TC: kt * T + (ch + 1) * TC],
                      s0=(kt == 0), s1=(kt == KD - 1):
                      eng.matmul(o[:], l, r, start=s0, stop=s1))
            pb.mark("tensor", "m_tensor")
            u_done = pb.cnt["m_tensor"]
            wbusy[sl] = u_done
            pb.wait("scalar", "m_tensor", g_done)
            if sg_busy:
                pb.wait("scalar", "m_vector", sg_busy)
            pb.op("scalar", lambda eng, o=sgate[:], i_=psg:
                  eng.activation(o, i_[:], AF.Silu, bias=0.0, scale=INV), inc="m_scalar")
            pb.wait("vector", "m_scalar", pb.cnt["m_scalar"])
            pb.wait("vector", "m_tensor", u_done)
            if hid_busy:
                pb.wait("vector", "m_tensor", hid_busy)
            pb.op("vector", lambda eng, o=hid[:, f16 * TC:(f16 + 1) * TC], i0=psu,
                  s1=INV, i1=sgate[:]:
                  eng.scalar_tensor_tensor(o, i0[:], s1, i1, ALU.mult, ALU.mult),
                  inc="m_vector")
            sg_busy = pb.cnt["m_vector"]
        hid_ready = pb.cnt["m_vector"]
        for fo in range(8):
            sl = 0
            wd = wdr[:]
            if wdbusy[sl]:
                pb.wait("sync", "m_tensor", wdbusy[sl])
            cw = pb.dma("sync", wd.rearrange("p (k f) -> p k f", k=24),
                        wdT3[:, :, fo * 128:(fo + 1) * 128].rearrange("k p f -> p k f"))
            pb.wait("tensor", "q_sync", cw)
            pb.wait("tensor", "m_vector", hid_ready)
            if gf_busy[0]:
                pb.wait("tensor", "m_scalar", gf_busy[0])
            for kt in range(24):
                pb.op("tensor", lambda eng, o=psd, l=wd[:, kt * 128:(kt + 1) * 128],
                      r=hid[:, kt * TC:(kt + 1) * TC], s0=(kt == 0), s1=(kt == 23):
                      eng.matmul(o[:], l, r, start=s0, stop=s1))
            pb.mark("tensor", "m_tensor")
            wdbusy[sl] = pb.cnt["m_tensor"]
            pb.wait("scalar", "m_tensor", pb.cnt["m_tensor"])
            if gf_busy[1]:
                pb.wait("scalar", "m_tensor", gf_busy[1])
            pb.op("scalar", lambda eng, o=gf[:], i_=psd:
                  eng.activation(o, i_[:], AF.Silu, bias=0.0, scale=INV), inc="m_scalar")
            gf_busy[0] = pb.cnt["m_scalar"]
            pb.wait("tensor", "m_scalar", pb.cnt["m_scalar"])
            for tt in range(4):
                tsl = tt % 2
                pst = (psTf0 if tsl == 0 else psTf1)[:]
                if S["_l1g_ps_busy"][tsl]:
                    pb.wait("tensor", "m_scalar", S["_l1g_ps_busy"][tsl])
                pb.op("tensor", lambda eng, o=pst, i_=gf[:, tt * 128:(tt + 1) * 128],
                      ey=S["eyef"]: eng.transpose(o, i_, ey[:]), inc="m_tensor")
                gf_busy[1] = pb.cnt["m_tensor"]
                pb.wait("scalar", "m_tensor", pb.cnt["m_tensor"])
                if fo == 0 and S["_gtm1_rd"][tt]:
                    pb.wait("scalar", "q_gpsimd", S["_gtm1_rd"][tt])
                pb.op("scalar", lambda eng, o=gtm[:, tt * D + fo * 128: tt * D + (fo + 1) * 128],
                      i_=pst: eng.copy(o, i_), inc="m_scalar")
                S["_l1g_ps_busy"][tsl] = pb.cnt["m_scalar"]
        pb.wait("gpsimd", "m_scalar", pb.cnt["m_scalar"])
        for tt in range(4):
            gt = ch * 4 + tt
            pb.dma("gpsimd", W["g_scr"][gt * 128:(gt + 1) * 128, :],
                   gtm[:, tt * D:(tt + 1) * D])
            S["_gtm1_rd"][tt] = pb.cnt["q_gpsimd"]
    pb.barrier()
    al.pop()


def emit_l1_ATTN(pb, al, S, b):
    al.push()
    psS = al.ps("psS", [128, 384], F32)
    psTb0 = al.ps("psTba0", [128, 128], BF16)
    psTb1 = al.ps("psTba1", [128, 128], BF16)
    psCtx = al.ps("psCtx", [64, 128], F32)
    sc = al.sb("scb", [128, 384], F32)
    pex = al.sb("pex", [128, 384], F32)
    att = al.sb("attb", [128, 384], BF16)
    attT = al.sb("attT", [128, 3 * 128], BF16)
    cols = al.sb("acols", [128, 8], F32)
    scale = 0.125
    ps_busy = 0
    sc_busy = 0
    pex_busy = 0
    att_busy = 0
    attT_busy = [0, 0]
    ctx_busy = 0
    for h in range(16):
        qt = h // 2
        po = 64 * (h % 2)
        for qb in range(NTT):
            t0 = qb * 128
            if qb == 0:
                w0, mi = 0, 0
            elif qb == NTT - 1:
                w0, mi = T - 384, 2
            else:
                w0, mi = t0 - 128, 1
            if ps_busy:
                pb.wait("tensor", "m_vector", ps_busy)
            pb.op("tensor", lambda eng, o=psS,
                  l=S["q_bf"][po:po + 64, qt * T + t0: qt * T + t0 + 128],
                  r=S["k_bf"][po:po + 64, qt * T + w0: qt * T + w0 + 384]:
                  eng.matmul(o[:], l, r, start=True, stop=True), inc="m_tensor")
            pb.wait("vector", "m_tensor", pb.cnt["m_tensor"])
            if sc_busy:
                pb.wait("vector", "m_scalar", sc_busy)
            pb.op("vector", lambda eng, o=sc[:], i0=psS,
                  m=S["masks"][:, mi * 384:(mi + 1) * 384]:
                  eng.scalar_tensor_tensor(o, i0[:], scale, m, ALU.mult, ALU.add),
                  inc="m_vector")
            ps_busy = pb.cnt["m_vector"]
            pb.dep("vector")
            pb.op("vector", lambda eng, o=cols[:, 0:1], i_=sc[:]:
                  eng.reduce_max(o, i_, axis=AX, negate=True), inc="m_vector")
            pb.wait("scalar", "m_vector", pb.cnt["m_vector"])
            if pex_busy:
                pb.wait("scalar", "m_vector", pex_busy)
            pb.op("scalar", lambda eng, o=pex[:], i_=sc[:], bi=cols[:, 0:1], a=cols[:, 1:2]:
                  eng.activation(o, i_, AF.Exp, bias=bi, scale=1.0, accum_out=a),
                  inc="m_scalar")
            sc_busy = pb.cnt["m_scalar"]
            pb.wait("vector", "m_scalar", pb.cnt["m_scalar"])
            pb.op("vector", lambda eng: eng.reciprocal(cols[:, 2:3], cols[:, 1:2]))
            pb.dep("vector")
            if att_busy:
                pb.wait("vector", "m_tensor", att_busy)
            pb.op("vector", lambda eng, o=att[:], i_=pex[:], s1=cols[:, 2:3]:
                  eng.tensor_scalar(o, i_, s1, None, ALU.mult), inc="m_vector")
            pex_busy = pb.cnt["m_vector"]
            pb.wait("tensor", "m_vector", pb.cnt["m_vector"])
            for j in range(3):
                tsl = j % 2
                pst = (psTb0 if tsl == 0 else psTb1)[:]
                if attT_busy[tsl]:
                    pb.wait("tensor", "m_scalar", attT_busy[tsl])
                pb.op("tensor", lambda eng, o=pst, i_=att[:, j * 128:(j + 1) * 128],
                      ey=S["eyeb"]: eng.transpose(o, i_, ey[:]), inc="m_tensor")
                pb.wait("scalar", "m_tensor", pb.cnt["m_tensor"])
                pb.op("scalar", lambda eng, o=attT[:, j * 128:(j + 1) * 128], i_=pst:
                      eng.copy(o, i_), inc="m_scalar")
                attT_busy[tsl] = pb.cnt["m_scalar"]
            att_busy = pb.cnt["m_tensor"]
            pb.wait("tensor", "m_scalar", pb.cnt["m_scalar"])
            if ctx_busy:
                pb.wait("tensor", "m_scalar", ctx_busy)
            for j in range(3):
                tb = (w0 + j * 128) // 128
                pb.op("tensor", lambda eng, o=psCtx,
                      l=S["vT"][:, tb * D + qt * 128 + po: tb * D + qt * 128 + po + 64],
                      r=attT[:, j * 128:(j + 1) * 128], s0=(j == 0), s1=(j == 2):
                      eng.matmul(o[:], l, r, start=s0, stop=s1))
            pb.mark("tensor", "m_tensor")
            pb.wait("scalar", "m_tensor", pb.cnt["m_tensor"])
            pb.op("scalar", lambda eng, o=S["ctx_bf"][po:po + 64, qt * T + t0: qt * T + t0 + 128],
                  i_=psCtx: eng.copy(o, i_[:]), inc="m_scalar")
            ctx_busy = pb.cnt["m_scalar"]
    pb.barrier()
    al.pop()


def emit_l1_PROJ(pb, al, S, W, x1ap, outap, b):
    al.push()
    psa = al.ps("pspj0", [128, TC], F32)
    psb = al.ps("pspj1", [128, TC], F32)
    psTf0 = al.ps("psTfp0", [128, 128], F32)
    psTf1 = al.ps("psTfp1", [128, 128], F32)
    wring = al.sb("wpring", [128, 2 * KD * 128], BF16)
    otm = al.sb("otm", [128, 4 * D], F32)
    oev = al.sb("oev", [128, 2 * TC], F32)
    xn1r = al.sb("xn1r", [128, 2 * D], F32)
    x1r = al.sb("x1r", [128, 2 * D], F32)
    gr = al.sb("gr", [128, 2 * D], F32)
    sgt = al.sb("sgt1", [128, D], F32)
    woutT3 = W["woutT_bf"].rearrange("(k p) f -> k p f", p=128)
    wbusy = [0, 0]
    psbusy = [0, 0]
    oev_busy = [0, 0]
    tb_busy = [0, 0]
    ld_busy = [0, 0]
    st_busy = [0, 0]
    for ch in range(NCH):
        otm_rd = S.get("_otm_rd", 0)
        for fo in range(8):
            sl = fo % 2
            wr = wring[:, sl * KD * 128:(sl + 1) * KD * 128]
            if wbusy[sl]:
                pb.wait("sync", "m_tensor", wbusy[sl])
            cw = pb.dma("sync", wr.rearrange("p (k f) -> p k f", k=KD),
                        woutT3[:, :, fo * 128:(fo + 1) * 128].rearrange("k p f -> p k f"))
            pb.wait("tensor", "q_sync", cw)
            ps = psa if sl == 0 else psb
            if psbusy[sl]:
                pb.wait("tensor", "m_vector", psbusy[sl])
            for kt in range(KD):
                pb.op("tensor", lambda eng, o=ps, l=wr[:, kt * 128:(kt + 1) * 128],
                      r=S["ctx_bf"][:, kt * T + ch * TC: kt * T + (ch + 1) * TC],
                      s0=(kt == 0), s1=(kt == KD - 1):
                      eng.matmul(o[:], l, r, start=s0, stop=s1))
            pb.mark("tensor", "m_tensor")
            wbusy[sl] = pb.cnt["m_tensor"]
            pb.wait("vector", "m_tensor", pb.cnt["m_tensor"])
            if oev_busy[sl]:
                pb.wait("vector", "m_tensor", oev_busy[sl])
            pb.op("vector", lambda eng, o=oev[:, sl * TC:(sl + 1) * TC], i0=ps,
                  s1=S["boutS"][:, fo:fo + 1]:
                  eng.tensor_scalar(o, i0[:], s1, None, ALU.add), inc="m_vector")
            psbusy[sl] = pb.cnt["m_vector"]
            pb.wait("tensor", "m_vector", pb.cnt["m_vector"])
            for tt in range(4):
                tsl = tt % 2
                pst = (psTf0 if tsl == 0 else psTf1)[:]
                if tb_busy[tsl]:
                    pb.wait("tensor", "m_scalar", tb_busy[tsl])
                pb.op("tensor", lambda eng, o=pst,
                      i_=oev[:, sl * TC + tt * 128: sl * TC + (tt + 1) * 128],
                      ey=S["eyef"]: eng.transpose(o, i_, ey[:]), inc="m_tensor")
                pb.wait("scalar", "m_tensor", pb.cnt["m_tensor"])
                if fo == 0 and otm_rd:
                    pb.wait("scalar", "m_vector", otm_rd)
                pb.op("scalar", lambda eng, o=otm[:, tt * D + fo * 128: tt * D + (fo + 1) * 128],
                      i_=pst: eng.copy(o, i_), inc="m_scalar")
                tb_busy[tsl] = pb.cnt["m_scalar"]
            oev_busy[sl] = pb.cnt["m_tensor"]
        pb.wait("vector", "m_scalar", pb.cnt["m_scalar"])
        for tt in range(4):
            gt = ch * 4 + tt
            sl = tt % 2
            xn1t = xn1r[:, sl * D:(sl + 1) * D]
            x1t = x1r[:, sl * D:(sl + 1) * D]
            grt = gr[:, sl * D:(sl + 1) * D]
            if ld_busy[sl]:
                pb.wait("sync", "m_vector", ld_busy[sl])
            if st_busy[sl]:
                pb.wait("sync", "q_gpsimd", st_busy[sl])
            c1 = pb.dma("sync", xn1t, W["xn1_scr"][gt * 128:(gt + 1) * 128, :])
            c2 = pb.dma("sync", x1t, x1ap[b, gt * 128:(gt + 1) * 128, :])
            c3 = pb.dma("sync", grt, W["g_scr"][gt * 128:(gt + 1) * 128, :])
            pb.wait("vector", "q_sync", c3)
            pb.op("vector", lambda eng, o=xn1t, i1=otm[:, tt * D:(tt + 1) * D]:
                  eng.tensor_tensor(o, o, i1, ALU.add))
            pb.dep("vector")
            pb.op("vector", lambda eng, o=sgt[:], i0=xn1t, i1=grt:
                  eng.tensor_tensor(o, i0, i1, ALU.mult))
            pb.dep("vector")
            pb.op("vector", lambda eng, o=x1t, i0=x1t, i1=sgt[:]:
                  eng.tensor_tensor(o, i0, i1, ALU.add), inc="m_vector")
            ld_busy[sl] = pb.cnt["m_vector"]
            pb.wait("gpsimd", "m_vector", pb.cnt["m_vector"])
            pb.dma("gpsimd", outap[b, gt * 128:(gt + 1) * 128, :], x1t)
            st_busy[sl] = pb.cnt["q_gpsimd"]
        S["_otm_rd"] = pb.cnt["m_vector"]
    pb.barrier()
    al.pop()



def build(stage=99, dbg=(), dump_chunk=0, moe_part=3, l1_only=False):
    nc = bass.Bass("TRN2", target_bir_lowering=False, debug=False)
    H = {"x": nc.dram_tensor("x", [BC, T, D], F32, kind="ExternalInput").ap()}
    specs = {
        "WmT": ([D, D], F32), "WpT": ([512, D], F32), "vv": ([128, 4], F32),
        "osc": ([128, 8], F32), "rtT": ([D, 8], F32),
        "guT": ([8, D, 2048], F32), "dpT": ([8, D, D], F32),
        "winT": ([D, 3072], F32), "binS": ([128, 24], F32),
        "woutT": ([D, D], F32), "boutS": ([128, 8], F32),
        "wgT": ([D, 3072], I32), "wuT": ([D, 3072], I32), "wdT": ([3072, D], I32),
        "eyeb": ([128, 128], BF16), "eyef": ([128, 128], F32),
        "onesf": ([1, 128], F32), "masks": ([3, 128, 384], F32),
    }
    for name, (shape, dt) in specs.items():
        H[name] = nc.dram_tensor(name, shape, dt, kind="ExternalInput").ap()
    out = nc.dram_tensor("out", [BC, T, D], F32, kind="ExternalOutput").ap()
    W = {
        "s_scr": nc.dram_tensor("s_scr", [T, D], F32, kind="Internal").ap(),
        "xn1_scr": nc.dram_tensor("xn1_scr", [T, D], F32, kind="Internal").ap(),
        "g_scr": nc.dram_tensor("g_scr", [T, D], F32, kind="Internal").ap(),
        "guT_bf": nc.dram_tensor("guT_bf", [8 * D, 2048], BF16, kind="Internal").ap(),
        "dpT_bf": nc.dram_tensor("dpT_bf", [8 * D, D], BF16, kind="Internal").ap(),
        "winT_bf": nc.dram_tensor("winT_bf", [D, 3072], BF16, kind="Internal").ap(),
        "woutT_bf": nc.dram_tensor("woutT_bf", [D, D], BF16, kind="Internal").ap(),
        "wgT_bf": nc.dram_tensor("wgT_bf", [D, 3072], BF16, kind="Internal").ap(),
        "wuT_bf": nc.dram_tensor("wuT_bf", [D, 3072], BF16, kind="Internal").ap(),
        "wdT_bf": nc.dram_tensor("wdT_bf", [3072, D], BF16, kind="Internal").ap(),
    }
    dbg_out = {}
    for name in dbg:
        shp = {"xn_fm": [128, KD * TC], "xn2_fm": [128, KD * TC],
               "stat": [128, 32], "s0": [TC, D], "s_full": [T, D],
               "pl": [128, 8 * TC], "upl": [128, 4 * TC],
               "ostm": [128, 4 * D], "hbf_d": [128, 4 * TC],
               "plf": [128, 8 * TC], "carry": [128, 4],
               "acc": [128, KD * TC], "comb": [128, 32],
               "xn1_full": [T, D], "g_full": [T, D], "ctx_full": [128, KD * T]}[name]
        dbg_out[name] = nc.dram_tensor("dbg_" + name, shp, F32,
                                       kind="ExternalOutput").ap()

    pb = PB()
    for s in ["q_sync", "q_gpsimd", "m_tensor", "m_vector", "m_scalar",
              "m_gpsimd", "m_sync"]:
        pb.new_sem(s)

    al = Alloc(nc)
    al.push()
    S = {}
    S["eyeb"] = al.sb("sb_eyeb", [128, 128], BF16)
    S["eyef"] = al.sb("sb_eyef", [128, 128], F32)
    S["onesf"] = al.sb("sb_onesf", [1, 128], F32)
    S["vv"] = al.sb("sb_vv", [128, 4], F32)
    S["osc"] = al.sb("sb_osc", [128, 8], F32)
    S["binS"] = al.sb("sb_binS", [128, 24], F32)
    S["boutS"] = al.sb("sb_boutS", [128, 8], F32)
    S["masks"] = al.sb("sb_masks", [128, 3 * 384], F32)
    S["epsc"] = al.sb("sb_epsc", [128, 2], F32)
    al.push()
    S["WmT"] = al.sb("sb_WmT", [128, KD * D], BF16)
    S["WpT"] = al.sb("sb_WpT", [128, 4 * D], BF16)
    S["rtT"] = al.sb("sb_rtT", [128, KD * 8], BF16)

    al.push()
    emit_prologue(pb, al, H, S)
    if stage >= 2 or l1_only:
        emit_weight_cache(pb, al, H, W)
    al.pop()

    al.push()
    S["xld"] = al.sb("xld", [128, 4 * D], F32)
    S["_xld_busy"] = [0, 0, 0, 0]
    S["stat"] = al.sb("stat", [128, 32], F32)
    S["sqdump"] = al.sb("sqdump", [128, D], F32)
    S["xn_tm"] = al.sb("xn_tm", [128, 4 * D], F32)
    S["xnbf"] = al.sb("xnbf", [128, 4 * D], BF16)
    S["xn2bf"] = al.sb("xn2bf", [128, 4 * D], BF16)
    S["xn_fm"] = al.sb("xn_fm", [128, KD * TC], BF16)
    S["xn2_fm"] = al.sb("xn2_fm", [128, KD * TC], BF16)
    S["carry"] = al.sb("carry", [128, 4], F32)
    S["psA"] = al.ps("psA", [128, TC], F32)
    S["psB"] = al.ps("psB", [128, TC], F32)
    S["psC"] = al.ps("psC", [128, TC], F32)
    S["psD"] = al.ps("psD", [128, TC], F32)
    S["psTf0"] = al.ps("psTf0", [128, 128], F32)
    S["psTf1"] = al.ps("psTf1", [128, 128], F32)
    S["psTb0"] = al.ps("psTb0", [128, 128], BF16)
    S["psTb1"] = al.ps("psTb1", [128, 128], BF16)
    S["acc"] = al.sb("acc", [128, KD * TC], F32)
    S["_psCD_busy"] = [0, 0]
    S["_psDn_busy"] = [0, 0]
    S["_comb_ld_busy"] = [0, 0]
    S["_l1g_ps_busy"] = [0, 0]
    S["_gtm1_rd"] = [0, 0, 0, 0]
    S["_otm_rd"] = 0
    S["_comb_st_busy"] = [0, 0]
    S["_gtm_busy"] = [0, 0]
    S["_gsil_busy"] = 0
    S["_pstsB"] = [0, 0]
    S["_psAB_busy"] = [0, 0]
    S["_psT_busy"] = [0, 0]
    S["_osout_busy"] = [0, 0]

    S["_dump_pl"] = dbg_out.get("pl")
    S["_dump_ostm"] = dbg_out.get("ostm")
    S["_dump_plf"] = dbg_out.get("plf")
    S["_dump_carry"] = dbg_out.get("carry")
    S["_dump_acc"] = dbg_out.get("acc")
    S["_dump_comb"] = dbg_out.get("comb")
    S["_dump_chunk"] = dump_chunk
    S["_dump_upl"] = dbg_out.get("upl")
    if stage == 4:
        pb.op("vector", lambda eng: eng.tensor_copy(S["carry"][:], S["vv"][:]),
              inc="m_vector")
        emit_l0_chunk_A(pb, S, H["x"], W["s_scr"], 0, 0)
        emit_l0_chunk_B(pb, al, S, W["s_scr"], 0, 0)
        emit_l0_chunk_MOE(pb, al, S, W, 0, 0, part=moe_part)
        pb.barrier()
        if "acc" in dbg_out:
            pb.dma("sync", dbg_out["acc"][:], S["acc"][:])
            pb.barrier()
    elif stage >= 5:
        for b in range(BC):
            pb.op("vector", lambda eng: eng.tensor_copy(S["carry"][:], S["vv"][:]),
                  inc="m_vector")
            for c in range(NCH):
                emit_l0_chunk_A(pb, S, H["x"], W["s_scr"], b, c)
                emit_l0_chunk_B(pb, al, S, W["s_scr"], b, c)
                emit_l0_chunk_MOE(pb, al, S, W, b, c)
                emit_l0_chunk_COMB(pb, al, S, H["x"], out, W, b, c)
        pb.barrier()
    elif stage >= 3:
        pb.op("vector", lambda eng: eng.tensor_copy(S["carry"][:], S["vv"][:]),
              inc="m_vector")
        for c in range(NCH):
            emit_l0_chunk_A(pb, S, H["x"], W["s_scr"], 0, c)
            emit_l0_chunk_B(pb, al, S, W["s_scr"], 0, c)
        pb.barrier()
    elif stage >= 1:
        emit_l0_chunk_A(pb, S, H["x"], W["s_scr"], 0, 0)
        pb.barrier()
        if "xn_fm" in dbg_out:
            stg = S["xn_tm"]
            for kt in range(KD):
                for which, sb_ in (("xn_fm", S["xn_fm"]), ("xn2_fm", S["xn2_fm"])):
                    if which not in dbg_out:
                        continue
                    pb.op("vector", lambda eng, o=stg[:, 0:TC], i=sb_[:, kt * TC:(kt + 1) * TC]:
                          eng.tensor_copy(o, i), inc="m_vector")
                    pb.wait("sync", "m_vector", pb.cnt["m_vector"])
                    c = pb.dma("sync", dbg_out[which][:, kt * TC:(kt + 1) * TC], stg[:, 0:TC])
                    pb.wait("vector", "q_sync", c)
        if "stat" in dbg_out:
            pb.dma("sync", dbg_out["stat"][:], S["stat"][:])
        if "s0" in dbg_out:
            pb.barrier()
            pb.dma("sync", dbg_out["s0"][:], W["s_scr"][0:TC, :])
        pb.barrier()
    if "s_full" in dbg_out:
        pb.barrier()
        pb.dma("sync", dbg_out["s_full"][:], W["s_scr"][:])
        pb.barrier()
    al.pop()
    al.pop()
    pb.barrier()

    if stage >= 6 or l1_only:
        x1ap = H["x"] if l1_only else out
        al.push()
        S["q_bf"] = al.sb("q_bf", [128, KD * T], BF16)
        S["k_bf"] = al.sb("k_bf", [128, KD * T], BF16)
        S["vT"] = al.sb("vT", [128, NTT * D], BF16)
        for b in range(BC):
            al.push()
            S["xn1_fm"] = al.sb("xn1_fm", [128, KD * T], BF16)
            al.push()
            S["xld"] = al.sb("xld1", [128, 4 * D], F32)
            S["stat"] = al.sb("stat1", [128, 32], F32)
            S["sqdump"] = al.sb("sqd1", [128, D], F32)
            S["xn_tm"] = al.sb("xn_tm1", [128, 4 * D], F32)
            S["xnbf"] = al.sb("xnbf1", [128, 4 * D], BF16)
            S["psTb0"] = al.ps("psTbA0", [128, 128], BF16)
            S["psTb1"] = al.ps("psTbA1", [128, 128], BF16)
            S["_xld_busy"] = [0, 0, 0, 0]
            S["_pstsB"] = [0, 0]
            S["_xn1_spill"] = 0
            emit_l1_A(pb, S, x1ap, W, b)
            pb.barrier()
            al.pop()
            if b == 0 and "xn1_full" in dbg_out:
                pb.dma("sync", dbg_out["xn1_full"][:], W["xn1_scr"][:])
                pb.barrier()
            if stage >= 7 or l1_only:
                emit_l1_QKV(pb, al, S, W, b)
            if stage >= 8 or l1_only:
                emit_l1_FFN(pb, al, S, W, b)
            if b == 0 and "g_full" in dbg_out:
                pb.dma("sync", dbg_out["g_full"][:], W["g_scr"][:])
                pb.barrier()
            pb.barrier()
            al.pop()
            if stage >= 9 or l1_only:
                al.push()
                S["ctx_bf"] = al.sb("ctx_bf", [128, KD * T], BF16)
                emit_l1_ATTN(pb, al, S, b)
                if b == 0 and "ctx_full" in dbg_out:
                    stgx = al.sb("ctxdump", [128, T], F32)
                    for kt in range(KD):
                        pb.op("vector", lambda eng, o=stgx[:], i=S["ctx_bf"][:, kt * T:(kt + 1) * T]:
                              eng.tensor_copy(o, i), inc="m_vector")
                        pb.wait("sync", "m_vector", pb.cnt["m_vector"])
                        cdd = pb.dma("sync", dbg_out["ctx_full"][:, kt * T:(kt + 1) * T], stgx[:])
                        pb.wait("vector", "q_sync", cdd)
                    pb.barrier()
                emit_l1_PROJ(pb, al, S, W, x1ap, out, b)
                al.pop()
        al.pop()
        pb.barrier()

    if stage < 5 and not l1_only:
        al.push()
        z = al.sb("zz", [128, D], F32)
        pb.op("vector", lambda eng, zz=z: eng.memset(zz[:], 0.0), inc="m_vector")
        pb.wait("sync", "m_vector", pb.cnt["m_vector"])
        for b in range(BC):
            for tt in range(NTT):
                pb.dma("sync", out[b, tt * 128:(tt + 1) * 128, :], z[:])
        pb.wait("sync", "q_sync", pb.cnt["q_sync"])
        al.pop()
    else:
        pb.wait("sync", "q_gpsimd", pb.cnt["q_gpsimd"])
        pb.wait("sync", "q_sync", pb.cnt["q_sync"])
    al.pop()

    semguards = []
    for name in list(pb.cnt.keys()):
        g = nc.semaphore(name)
        pb.sems[name] = g.__enter__()
        semguards.append(g)

    with nc.Block() as block:
        @block.sync
        def _(eng):
            pb.emit("sync", eng)

        @block.tensor
        def _(eng):
            pb.emit("tensor", eng)

        @block.vector
        def _(eng):
            pb.emit("vector", eng)

        @block.scalar
        def _(eng):
            pb.emit("scalar", eng)

        @block.gpsimd
        def _(eng):
            pb.emit("gpsimd", eng)

    for g in reversed(semguards):
        g.__exit__(None, None, None)
    return nc, dbg_out


def kernel(**inputs):
    Hn = host_prep(inputs)
    x = np.ascontiguousarray(np.asarray(inputs["L0_attn_x"], dtype=np.float32))
    nc, _ = build(stage=99)
    in_maps = []
    for core in range(8):
        m = {k: Hn[k] for k in HOST_KEYS}
        m["x"] = np.ascontiguousarray(x[core * BC:(core + 1) * BC])
        in_maps.append(m)
    res = run_bass_kernel_spmd(nc, in_maps, core_ids=list(range(8)))
    return np.concatenate([res.results[c]["out"] for c in range(8)], axis=0)
